# version 01 (final)
# baseline (speedup 1.0000x reference)
import numpy as np
import ml_dtypes
from contextlib import ExitStack

import concourse.bass as bass
import concourse.mybir as mybir

F32 = mybir.dt.float32
BF16 = mybir.dt.bfloat16
I32 = mybir.dt.int32
AF = mybir.ActivationFunctionType
ALU = mybir.AluOpType
AX = mybir.AxisListType

NBF = ml_dtypes.bfloat16

B, S, E = 2, 4096, 2048
NCORE = 8
TPC = S // 4
H, D = 16, 128
HPC = 4


class Ctx:

    def __init__(self, nc, stack):
        self.nc = nc
        self.stack = stack
        self._sems = {}
        self.cnt = {}

    def sem(self, name):
        if name not in self._sems:
            self._sems[name] = self.stack.enter_context(self.nc.semaphore(name))
            self.cnt[name] = 0
        return self._sems[name]

    def inc(self, inst, name, n=1):
        inst.then_inc(self.sem(name), n)
        self.cnt[name] += n
        return inst

    POOL = 8

    def dma(self, eng, out, in_, name=None):
        if name is None:
            cls = "lg" if eng is self.nc.gpsimd else "ld"
        else:
            cls = "sg" if (name == "st" and eng is self.nc.gpsimd) else \
                  ("st" if name == "st" else name)
        if not hasattr(self, "_rrp"):
            self._rrp = {}
        i = self._rrp.get(cls, 0)
        self._rrp[cls] = (i + 1) % Ctx.POOL
        self.inc(eng.dma_start(out=out, in_=in_), f"{cls}{i}", 16)

    def mark(self, classes):
        return {f"{cl}{i}": self.cnt.get(f"{cl}{i}", 0)
                for cl in classes for i in range(Ctx.POOL)}

    def waitmark(self, eng, mark):
        for k, v in mark.items():
            if v:
                self.wait(eng, k, v)

    def ldmark(self):
        return self.mark(("ld", "lg"))

    def ldwait(self, eng, mark=None):
        self.waitmark(eng, mark if mark is not None else self.ldmark())

    def stmark(self):
        return self.mark(("st", "sg"))

    def stwait(self, eng, mark=None):
        self.waitmark(eng, mark if mark is not None else self.stmark())

    def fence(self):
        nc = self.nc
        streams = [k for k in self.cnt if self.cnt[k]]
        for eng in (nc.tensor, nc.vector, nc.scalar, nc.gpsimd, nc.sync):
            for s in streams:
                self.wait(eng, s, self.cnt[s])

    def gwait(self, eng, guard):
        s_, v_ = guard
        if s_ == "__mark__":
            self.waitmark(eng, v_)
        elif s_:
            self.wait(eng, s_, v_)

    def wait(self, eng, name, val=None):
        self.sem(name)
        v = self.cnt[name] if val is None else val
        eng.wait_ge(self._sems[name], v)

    _uid = [0]

    def sb(self, name, shape, dt):
        Ctx._uid[0] += 1
        return self.stack.enter_context(
            self.nc.sbuf_tensor(f"{name}_{Ctx._uid[0]}", shape, dt))

    def constcol(self, value, parts=128):
        key = (value, parts)
        if not hasattr(self, "_cc"):
            self._cc = {}
        if key not in self._cc:
            t = self.sb(f"cc{len(self._cc)}", [parts, 1], F32)
            self.v("memset", t[:, :], float(value))
            self._cc[key] = t
        return self._cc[key][:, :]

    def ps(self, name, shape, dt=F32):
        return self.stack.enter_context(self.nc.psum_tensor(name, shape, dt))

    def v(self, method, *args, **kw):
        self.wait(self.nc.vector, "v")
        return self.inc(getattr(self.nc.vector, method)(*args, **kw), "v")

    def a(self, method, *args, **kw):
        self.wait(self.nc.scalar, "a")
        return self.inc(getattr(self.nc.scalar, method)(*args, **kw), "a")

    def mm(self, *args, **kw):
        return self.inc(self.nc.tensor.matmul(*args, **kw), "mm")

    def mmq(self, *args, **kw):
        return self.nc.tensor.matmul(*args, **kw)


def fm_slab(dram, rows, cols, p=128):
    return dram[rows, cols].rearrange("(c p) n -> p c n", p=p)


def build_k1():
    nc = bass.Bass("TRN2", target_bir_lowering=False)
    T = TPC
    x_fm = nc.dram_tensor("x_fm", [E, T], F32, kind="ExternalInput")
    wq = nc.dram_tensor("wq", [E, E], F32, kind="ExternalInput")
    wk = nc.dram_tensor("wk", [E, E], F32, kind="ExternalInput")
    wv = nc.dram_tensor("wv", [E, E], F32, kind="ExternalInput")
    wgT = nc.dram_tensor("wgT", [E, 8192], I32, kind="ExternalInput")
    wuT = nc.dram_tensor("wuT", [E, 8192], I32, kind="ExternalInput")
    wdT = nc.dram_tensor("wdT", [8192, E], I32, kind="ExternalInput")
    gsc = nc.dram_tensor("gsc", [128, 64], F32, kind="ExternalInput")
    usc = nc.dram_tensor("usc", [128, 64], F32, kind="ExternalInput")
    dsc = nc.dram_tensor("dsc", [128, 16], F32, kind="ExternalInput")

    q_o = nc.dram_tensor("q_o", [E, T], BF16, kind="ExternalOutput")
    k_o = nc.dram_tensor("k_o", [E, T], BF16, kind="ExternalOutput")
    v_o = nc.dram_tensor("v_o", [E, T], BF16, kind="ExternalOutput")
    gs_o = nc.dram_tensor("gs_o", [E, T], BF16, kind="ExternalOutput")

    with ExitStack() as stack:
        c = Ctx(nc, stack)
        pb = [c.ps(f"pb{i}", [128, 512]) for i in range(8)]
        bank_guard = [("", 0)] * 8

        def bank_wait(i):
            c.gwait(nc.tensor, bank_guard[i])

        xb = c.sb("xb", [128, 16 * T], BF16)
        ones_c = c.sb("ones_c", [128, 1], BF16)
        ones_r = c.sb("ones_r", [1, 128], BF16)
        n128_r = c.sb("n128_r", [1, 128], BF16)
        scr = c.sb("scr", [1, 4 * T], F32)
        row_bf = c.sb("row_bf", [1, 2 * T], BF16)
        MU = c.sb("MU", [128, T], BF16)
        RSTD = c.sb("RSTD", [128, T], BF16)
        csum = c.sb("csum", [1, T], BF16)
        gs_sb = c.sb("gs_sb", [128, 64], F32)
        us_sb = c.sb("us_sb", [128, 64], F32)
        ds_sb = c.sb("ds_sb", [128, 16], F32)

        c.v("memset", ones_c[:, :], 1.0)
        c.v("memset", ones_r[:, :], 1.0)
        c.v("memset", n128_r[:, :], -128.0)
        c.constcol(1e-5, 1)
        c.dma(nc.sync, gs_sb[:, :], gsc[:, :])
        c.dma(nc.sync, us_sb[:, :], usc[:, :])
        c.dma(nc.sync, ds_sb[:, :], dsc[:, :])

        with ExitStack() as st1:
            c1 = Ctx(nc, st1)
            c1._sems, c1.cnt = c._sems, c.cnt
            x_sb = c1.sb("x_sb", [128, 16 * T], F32)
            sq = c1.sb("sq", [128, 2 * T], BF16)
            tmpf = c1.sb("tmpf", [128, T], F32)
            c.dma(nc.sync, x_sb[:, :], fm_slab(x_fm, slice(0, E), slice(0, T)))
            ldc = c.ldmark()
            for ci in range(16):
                xc = x_sb[:, ci * T:(ci + 1) * T]
                xbc = xb[:, ci * T:(ci + 1) * T]
                sqc = sq[:, (ci % 2) * T:(ci % 2) * T + T]
                c.ldwait(nc.vector, ldc)
                if ci >= 2:
                    c.wait(nc.scalar, "mm")
                c.v("tensor_copy", xbc, xc)
                c.wait(nc.scalar, "v")
                c.a("activation", sqc, xbc, AF.Square)
                c.wait(nc.tensor, "v")
                c.wait(nc.tensor, "a")
                for th in range(2):
                    c.mmq(pb[th][0:1, :], ones_c[:, :],
                          xbc[:, th * 512:(th + 1) * 512],
                          start=(ci == 0), stop=(ci == 15))
                    m = c.mmq(pb[2 + th][0:1, :], ones_c[:, :],
                              sqc[:, th * 512:(th + 1) * 512],
                              start=(ci == 0), stop=(ci == 15))
                c.inc(m, "mm")
            s1 = scr[0:1, 0:T]
            s2 = scr[0:1, T:2 * T]
            mu = scr[0:1, 2 * T:3 * T]
            var = scr[0:1, 3 * T:4 * T]
            c.wait(nc.vector, "mm")
            for th in range(2):
                c.v("tensor_scalar", mu[:, th * 512:(th + 1) * 512],
                    pb[th][0:1, :], 1.0 / E, None, ALU.mult)
                c.v("tensor_scalar", s2[:, th * 512:(th + 1) * 512],
                    pb[2 + th][0:1, :], 1.0 / E, None, ALU.mult)
            c.wait(nc.vector, "v")
            c.v("tensor_mul", s1[:, :], mu[:, :], mu[:, :])
            c.wait(nc.vector, "v")
            c.v("tensor_sub", var[:, :], s2[:, :], s1[:, :])
            c.wait(nc.scalar, "v")
            c.a("activation", var[:, :], var[:, :], AF.Ln,
                bias=c.constcol(1e-5, 1))
            c.wait(nc.scalar, "a")
            c.a("activation", var[:, :], var[:, :], AF.Exp, scale=-0.5)
            mu_bf = row_bf[0:1, 0:T]
            rstd_bf = row_bf[0:1, T:2 * T]
            c.wait(nc.vector, "a")
            c.v("tensor_copy", mu_bf, mu[:, :])
            c.v("tensor_copy", rstd_bf, var[:, :])
            c.wait(nc.tensor, "v")
            for th in range(2):
                c.mmq(pb[4 + th][:, :], ones_r[:, :],
                      mu_bf[:, th * 512:(th + 1) * 512], start=True, stop=True)
                m = c.mmq(pb[6 + th][:, :], ones_r[:, :],
                          rstd_bf[:, th * 512:(th + 1) * 512], start=True, stop=True)
            c.inc(m, "mm")
            c.wait(nc.vector, "mm")
            c.v("tensor_copy", MU[:, 0:512], pb[4][:, :])
            c.v("tensor_copy", MU[:, 512:1024], pb[5][:, :])
            c.v("tensor_copy", RSTD[:, 0:512], pb[6][:, :])
            c.v("tensor_copy", RSTD[:, 512:1024], pb[7][:, :])
            for ci in range(16):
                xc = x_sb[:, ci * T:(ci + 1) * T]
                xbc = xb[:, ci * T:(ci + 1) * T]
                c.v("tensor_sub", tmpf[:, :], xc, MU[:, :])
                c.v("tensor_mul", xbc, tmpf[:, :], RSTD[:, :])
                c.wait(nc.tensor, "v")
                for th in range(2):
                    m = c.mmq(pb[th][0:1, :], ones_c[:, :],
                              xbc[:, th * 512:(th + 1) * 512],
                              start=(ci == 0), stop=(ci == 15))
            c.inc(m, "mm")
            c.wait(nc.vector, "mm")
            c.v("tensor_copy", csum[:, 0:512], pb[0][0:1, :])
            c.v("tensor_copy", csum[:, 512:1024], pb[1][0:1, :])
            for i in range(8):
                bank_guard[i] = ("v", c.cnt["v"])
            c.fence()

        with ExitStack() as st2:
            c2 = Ctx(nc, st2)
            c2._sems, c2.cnt = c._sems, c.cnt
            wbuf = c2.sb("wbuf", [128, 2 * 2048], BF16)
            qst = c2.sb("qst", [128, 2 * T], BF16)
            par_guard = [("", 0), ("", 0)]
            st_guard = [("", 0), ("", 0)]
            rr = [0]

            def acq():
                i = rr[0]
                rr[0] = (rr[0] + 1) % 8
                bank_wait(i)
                return i

            for wi, (w_in, w_out) in enumerate([(wq, q_o), (wk, k_o), (wv, v_o)]):
                for ct in range(16):
                    p = (wi * 16 + ct) % 2
                    c.gwait(nc.gpsimd, par_guard[p])
                    c.dma(nc.gpsimd, wbuf[:, p * 2048:(p + 1) * 2048],
                          fm_slab(w_in, slice(0, E),
                                  slice(ct * 128, (ct + 1) * 128)))
                    ldc = c.ldmark()
                    c.gwait(nc.vector, st_guard[p])
                    for th in range(2):
                        bi = acq()
                        c.ldwait(nc.tensor, ldc)
                        for dc in range(16):
                            m = c.mmq(
                                pb[bi][:, :],
                                wbuf[:, p * 2048 + dc * 128:
                                     p * 2048 + (dc + 1) * 128],
                                xb[:, dc * T + th * 512:dc * T + (th + 1) * 512],
                                start=(dc == 0), stop=(dc == 15))
                        c.inc(m, "mm")
                        c.wait(nc.vector, "mm")
                        c.v("tensor_copy",
                            qst[:, p * T + th * 512:p * T + (th + 1) * 512],
                            pb[bi][:, :])
                        bank_guard[bi] = ("v", c.cnt["v"])
                    par_guard[p] = ("mm", c.cnt["mm"])
                    c.wait(nc.sync, "v")
                    c.dma(nc.sync, w_out[ct * 128:(ct + 1) * 128, :],
                          qst[:, p * T:(p + 1) * T], name="st")
                    st_guard[p] = ("__mark__", c.stmark())
            c.fence()

        h_all = c.sb("h_all", [128, 64 * T], BF16)
        with ExitStack() as st3:
            c3 = Ctx(nc, st3)
            c3._sems, c3.cnt = c._sems, c.cnt
            wg_sb = c3.sb("wg_sb", [128, 2 * 2048], BF16)
            wu_sb = c3.sb("wu_sb", [128, 2 * 2048], BF16)
            gact = c3.sb("gact", [128, 2 * 512], BF16)
            par_guard = [("", 0), ("", 0)]
            rr = [0]

            def acq6():
                i = rr[0]
                rr[0] = (rr[0] + 1) % 6
                bank_wait(i)
                return i

            for ic in range(64):
                p = ic % 2
                c.gwait(nc.gpsimd, par_guard[p])
                c.dma(nc.gpsimd, wg_sb[:, p * 2048:(p + 1) * 2048],
                      fm_slab(wgT, slice(0, E), slice(ic * 128, (ic + 1) * 128)))
                c.dma(nc.gpsimd, wu_sb[:, p * 2048:(p + 1) * 2048],
                      fm_slab(wuT, slice(0, E), slice(ic * 128, (ic + 1) * 128)))
                ldc = c.ldmark()
                for th in range(2):
                    bg = acq6()
                    bu = acq6()
                    c.ldwait(nc.tensor, ldc)
                    for dc in range(16):
                        c.mmq(pb[bg][:, :],
                              wg_sb[:, p * 2048 + dc * 128:
                                    p * 2048 + (dc + 1) * 128],
                              xb[:, dc * T + th * 512:dc * T + (th + 1) * 512],
                              start=(dc == 0), stop=False)
                    c.mmq(pb[bg][:, :], n128_r[:, :],
                          csum[:, th * 512:(th + 1) * 512], start=False, stop=True)
                    for dc in range(16):
                        c.mmq(pb[bu][:, :],
                              wu_sb[:, p * 2048 + dc * 128:
                                    p * 2048 + (dc + 1) * 128],
                              xb[:, dc * T + th * 512:dc * T + (th + 1) * 512],
                              start=(dc == 0), stop=False)
                    m = c.mmq(pb[bu][:, :], n128_r[:, :],
                              csum[:, th * 512:(th + 1) * 512],
                              start=False, stop=True)
                    c.inc(m, "mm")
                    c.wait(nc.scalar, "mm")
                    if ic >= 1:
                        c.wait(nc.scalar, "v")
                    c.a("activation", gact[:, th * 512:(th + 1) * 512],
                        pb[bg][:, :], AF.Silu, scale=gs_sb[:, ic:ic + 1])
                    c.wait(nc.vector, "a")
                    hop = h_all[:, ic * T + th * 512: ic * T + (th + 1) * 512]
                    c.v("scalar_tensor_tensor", hop, pb[bu][:, :],
                        us_sb[:, ic:ic + 1], gact[:, th * 512:(th + 1) * 512],
                        ALU.mult, ALU.mult)
                    bank_guard[bg] = ("a", c.cnt["a"])
                    bank_guard[bu] = ("v", c.cnt["v"])
                    c.wait(nc.tensor, "v")
                    if ic == 0:
                        bank_wait(6 + th)
                    m = c.mmq(pb[6 + th][0:1, :], ones_c[:, :], hop,
                              start=(ic == 0), stop=(ic == 63))
                    if ic == 63:
                        c.inc(m, "mm")
                par_guard[p] = ("mm", c.cnt["mm"])
            c.wait(nc.vector, "mm")
            csh = scr[0:1, 0:T]
            c.v("tensor_copy", csh[:, 0:512], pb[6][0:1, :])
            c.v("tensor_copy", csh[:, 512:1024], pb[7][0:1, :])
            c.wait(nc.vector, "v")
            c.v("tensor_copy", row_bf[0:1, 0:T], csh[:, :])
            c.wait(nc.tensor, "v")
            for th in range(2):
                m = c.mmq(pb[4 + th][:, :], ones_r[:, :],
                          row_bf[0:1, th * 512:(th + 1) * 512],
                          start=True, stop=True)
            c.inc(m, "mm")
            c.wait(nc.vector, "mm")
            BH = MU
            c.v("tensor_copy", BH[:, 0:512], pb[4][:, :])
            c.v("tensor_copy", BH[:, 512:1024], pb[5][:, :])
            c.fence()

        with ExitStack() as st4:
            c4 = Ctx(nc, st4)
            c4._sems, c4.cnt = c._sems, c.cnt
            wd_sb = c4.sb("wd_sb", [128, 2 * 1024], BF16)
            dtmp = c4.sb("dtmp", [128, 2 * 512], F32)
            gstg = c4.sb("gstg", [128, 2 * 512], BF16)
            par_guard = [("", 0), ("", 0)]
            dg_guard = [("", 0), ("", 0)]
            sg_guard = [("", 0), ("", 0)]
            for th in range(2):
                for dh in range(2):
                    for i in range(8):
                        bank_wait(i)
                    for ic in range(64):
                        p = ic % 2
                        c.gwait(nc.gpsimd, par_guard[p])
                        c.dma(nc.gpsimd, wd_sb[:, p * 1024:(p + 1) * 1024],
                              wdT[ic * 128:(ic + 1) * 128,
                                  dh * 1024:(dh + 1) * 1024])
                        c.ldwait(nc.tensor)
                        for dt in range(8):
                            m = c.mmq(
                                pb[dt][:, :],
                                wd_sb[:, p * 1024 + dt * 128:
                                      p * 1024 + (dt + 1) * 128],
                                h_all[:, ic * T + th * 512:
                                      ic * T + (th + 1) * 512],
                                start=(ic == 0), stop=(ic == 63))
                        c.inc(m, "mm")
                        par_guard[p] = ("mm", c.cnt["mm"])
                    c.wait(nc.vector, "mm")
                    for dt in range(8):
                        d_abs = dh * 8 + dt
                        pg = dt % 2
                        c.gwait(nc.vector, dg_guard[pg])
                        c.v("scalar_tensor_tensor",
                            dtmp[:, pg * 512:(pg + 1) * 512],
                            BH[:, th * 512:(th + 1) * 512], -128.0,
                            pb[dt][:, :], ALU.mult, ALU.add)
                        bank_guard[dt] = ("v", c.cnt["v"])
                        c.wait(nc.scalar, "v")
                        c.gwait(nc.scalar, sg_guard[pg])
                        c.a("activation", gstg[:, pg * 512:(pg + 1) * 512],
                            dtmp[:, pg * 512:(pg + 1) * 512], AF.Silu,
                            scale=ds_sb[:, d_abs:d_abs + 1])
                        dg_guard[pg] = ("a", c.cnt["a"])
                        c.wait(nc.sync, "a")
                        c.dma(nc.sync, gs_o[d_abs * 128:(d_abs + 1) * 128,
                                            th * 512:(th + 1) * 512],
                              gstg[:, pg * 512:(pg + 1) * 512], name="st")
                        sg_guard[pg] = ("__mark__", c.stmark())
            c.stwait(nc.sync)
    return nc


def host_prep_k1(inputs):
    x = np.asarray(inputs["L0_attn_x"], dtype=np.float32)
    wq = np.asarray(inputs["L0_attn_wq"], dtype=np.float32)
    wk = np.asarray(inputs["L0_attn_wk"], dtype=np.float32)
    wv = np.asarray(inputs["L0_attn_wv"], dtype=np.float32)
    wgT = np.ascontiguousarray(np.asarray(inputs["L0_ffn_gate_w"]).T)
    wuT = np.ascontiguousarray(np.asarray(inputs["L0_ffn_up_w"]).T)
    wdT = np.ascontiguousarray(np.asarray(inputs["L0_ffn_down_w"]).T)
    gsc = np.ascontiguousarray(
        np.asarray(inputs["L0_ffn_gate_scale"], np.float32).reshape(64, 128).T)
    usc = np.ascontiguousarray(
        np.asarray(inputs["L0_ffn_up_scale"], np.float32).reshape(64, 128).T)
    dsc = np.ascontiguousarray(
        np.asarray(inputs["L0_ffn_down_scale"], np.float32).reshape(16, 128).T)
    maps = []
    for ci in range(NCORE):
        b, qq = divmod(ci, 4)
        x_fm = np.ascontiguousarray(x[b, qq * TPC:(qq + 1) * TPC, :].T)
        maps.append(dict(x_fm=x_fm, wq=wq, wk=wk, wv=wv, wgT=wgT, wuT=wuT,
                         wdT=wdT, gsc=gsc, usc=usc, dsc=dsc))
    return maps


def build_k2():
    nc = bass.Bass("TRN2", target_bir_lowering=False)
    q_i = nc.dram_tensor("q_i", [HPC * 128, S], BF16, kind="ExternalInput")
    k_i = nc.dram_tensor("k_i", [HPC * 128, S], BF16, kind="ExternalInput")
    v_i = nc.dram_tensor("v_i", [128, HPC * S], BF16, kind="ExternalInput")
    rc = nc.dram_tensor("rc", [128, S], F32, kind="ExternalInput")
    rs = nc.dram_tensor("rs", [128, S], F32, kind="ExternalInput")
    rcq = nc.dram_tensor("rcq", [128, S], F32, kind="ExternalInput")
    rsq = nc.dram_tensor("rsq", [128, S], F32, kind="ExternalInput")
    cmask = nc.dram_tensor("cmask", [128, 128], F32, kind="ExternalInput")
    ident = nc.dram_tensor("ident", [128, 128], BF16, kind="ExternalInput")
    ao_o = nc.dram_tensor("ao_o", [HPC * 128, S], BF16, kind="ExternalOutput")

    NQT = S // 128
    with ExitStack() as stack:
        c = Ctx(nc, stack)
        s_ps = [c.ps("sps0", [128, 512]), c.ps("sps1", [128, 512])]
        pt_ps = [c.ps("ptps0", [128, 1024], BF16), c.ps("ptps1", [128, 1024], BF16)]
        o_ps = c.ps("ops", [128, 512])
        st_ps = c.ps("stps", [128, 512])
        s_guard = [("", 0), ("", 0)]
        pt_guard = [("", 0), ("", 0)]
        o_guard = ("", 0)

        cosb = c.sb("cosb", [128, S], F32)
        sinb = c.sb("sinb", [128, S], F32)
        cosq = c.sb("cosq", [128, S], F32)
        sinq = c.sb("sinq", [128, S], F32)
        rp_s = c.sb("rp_s", [128, S], BF16)
        cm_sb = c.sb("cm_sb", [128, 128], F32)
        id_sb = c.sb("id_sb", [128, 128], BF16)
        qh = c.sb("qh", [128, S], BF16)
        kh = c.sb("kh", [128, S], BF16)
        q_raw = c.sb("q_raw", [128, S], BF16)
        k_raw = c.sb("k_raw", [128, S], BF16)
        vh = c.sb("vh", [128, S], BF16)
        P = c.sb("P", [128, S], BF16)
        pt_sb = c.sb("pt_sb", [128, 2 * 128], BF16)
        ao = c.sb("ao", [128, S], BF16)
        mx = c.sb("mx", [128, 16], F32)
        nm = c.sb("nm", [128, 1], F32)
        sums = c.sb("sums", [128, 16], F32)
        l_r = c.sb("l_r", [128, 1], F32)

        c.dma(nc.sync, cosb[:, :], rc[:, :])
        c.dma(nc.sync, sinb[:, :], rs[:, :])
        c.dma(nc.sync, cosq[:, :], rcq[:, :])
        c.dma(nc.sync, sinq[:, :], rsq[:, :])
        c.dma(nc.sync, cm_sb[:, :], cmask[:, :])
        c.dma(nc.sync, id_sb[:, :], ident[:, :])

        for h in range(HPC):
            hs = slice(h * 128, (h + 1) * 128)
            c.wait(nc.sync, "mm")
            c.wait(nc.sync, "v")
            c.stwait(nc.vector)
            c.dma(nc.sync, q_raw[:, :], q_i[hs, :])
            c.dma(nc.sync, k_raw[:, :], k_i[hs, :])
            c.dma(nc.sync, vh[:, :], v_i[:, h * S:(h + 1) * S])
            c.ldwait(nc.vector)
            for (src, dst, co, si) in ((q_raw, qh, cosq, sinq),
                                       (k_raw, kh, cosb, sinb)):
                s1, s2 = src[0:64, :], src[64:128, :]
                s2c, s1c = rp_s[0:64, :], rp_s[64:128, :]
                tmp0 = P[0:64, :]
                tmp1 = P[64:128, :]
                c.v("tensor_copy", s2c, s2)
                c.v("tensor_copy", s1c, s1)
                c.v("tensor_mul", tmp0, s2c, si[0:64, :])
                c.v("tensor_mul", dst[0:64, :], s1, co[0:64, :])
                c.v("tensor_sub", dst[0:64, :], dst[0:64, :], tmp0)
                c.v("tensor_mul", tmp1, s1c, si[64:128, :])
                c.v("tensor_mul", dst[64:128, :], s2, co[64:128, :])
                c.v("tensor_add", dst[64:128, :], dst[64:128, :], tmp1)
            c.wait(nc.tensor, "v")

            for qt in range(NQT):
                klen = (qt + 1) * 128
                nch = (klen + 511) // 512
                qsl = slice(qt * 128, (qt + 1) * 128)
                for ci in range(nch):
                    kw = min(512, klen - ci * 512)
                    sp = ci % 2
                    sname, sv = s_guard[sp]
                    if sname:
                        c.wait(nc.tensor, sname, sv)
                    m = c.mmq(s_ps[sp][:, 0:kw], qh[:, qsl],
                              kh[:, ci * 512:ci * 512 + kw],
                              start=True, stop=True)
                    c.inc(m, "mm")
                    c.wait(nc.vector, "mm")
                    c.v("tensor_reduce", mx[:, ci:ci + 1], s_ps[sp][:, 0:kw],
                        AX.X, ALU.max)
                    s_guard[sp] = ("v", c.cnt["v"])
                c.v("tensor_reduce", nm[:, :], mx[:, 0:nch], AX.X, ALU.max,
                    negate=True)
                c.wait(nc.scalar, "v")
                for ci in range(nch):
                    kw = min(512, klen - ci * 512)
                    sp = ci % 2
                    sname, sv = s_guard[sp]
                    if sname:
                        c.wait(nc.tensor, sname, sv)
                    m = c.mmq(s_ps[sp][:, 0:kw], qh[:, qsl],
                              kh[:, ci * 512:ci * 512 + kw],
                              start=True, stop=True)
                    c.inc(m, "mm")
                    if ci == nch - 1:
                        c.wait(nc.vector, "mm")
                        c.v("tensor_add", s_ps[sp][:, kw - 128:kw],
                            s_ps[sp][:, kw - 128:kw], cm_sb[:, :])
                        c.wait(nc.scalar, "v")
                    c.wait(nc.scalar, "mm")
                    c.a("activation", P[:, ci * 512:ci * 512 + kw],
                        s_ps[sp][:, 0:kw], AF.Exp, bias=nm[:, :],
                        accum_out=sums[:, ci:ci + 1])
                    s_guard[sp] = ("a", c.cnt["a"])
                c.wait(nc.vector, "a")
                c.v("tensor_reduce", l_r[:, :], sums[:, 0:nch], AX.X, ALU.add)
                c.v("reciprocal", l_r[:, :], l_r[:, :])
                c.v("tensor_scalar", P[:, 0:klen], P[:, 0:klen], l_r[:, :],
                    None, ALU.mult)
                on, ov = o_guard
                if on:
                    c.wait(nc.tensor, on, ov)
                c.wait(nc.tensor, "v")
                nkc = klen // 128
                for kc in range(nkc):
                    pp = kc % 2
                    pn_, pv_ = pt_guard[pp]
                    if pn_:
                        c.wait(nc.tensor, pn_, pv_)
                    m = nc.tensor.transpose(pt_ps[pp][:, 0:128],
                                            P[:, kc * 128:(kc + 1) * 128],
                                            id_sb[:, :])
                    c.inc(m, "mm")
                    c.wait(nc.vector, "mm")
                    c.v("tensor_copy", pt_sb[:, pp * 128:(pp + 1) * 128],
                        pt_ps[pp][:, 0:128])
                    pt_guard[pp] = ("v", c.cnt["v"])
                    c.wait(nc.tensor, "v")
                    m = c.mmq(o_ps[:, 0:128], vh[:, kc * 128:(kc + 1) * 128],
                              pt_sb[:, pp * 128:(pp + 1) * 128],
                              start=(kc == 0), stop=(kc == nkc - 1))
                    c.inc(m, "mm")
                c.wait(nc.vector, "mm")
                c.v("tensor_copy", ao[:, qsl], o_ps[:, 0:128])
                o_guard = ("v", c.cnt["v"])
            c.wait(nc.sync, "v")
            c.dma(nc.sync, ao_o[hs, :], ao[:, :], name="st")
        c.stwait(nc.sync)
    return nc


def rope_tables_np():
    half = 64
    pos = np.arange(S, dtype=np.float32)
    inv = 1.0 / 10000.0 ** (np.arange(0, D, 2, dtype=np.float32) / D)
    ang = pos[None, :] * inv[:, None]
    return np.sin(ang), np.cos(ang)


def host_prep_k2(k1_results):
    sin0, cos0 = rope_tables_np()
    sin = np.ascontiguousarray(np.concatenate([sin0, sin0], axis=0))
    cos = np.ascontiguousarray(np.concatenate([cos0, cos0], axis=0))
    sq = np.ascontiguousarray(sin / np.sqrt(D))
    cq = np.ascontiguousarray(cos / np.sqrt(D))
    cmask = np.triu(np.full((128, 128), -1e30, dtype=np.float32), k=1)
    ident = np.eye(128, dtype=NBF)
    maps = []
    for b in range(B):
        qf = np.concatenate([k1_results[b * 4 + i]["q_o"] for i in range(4)],
                            axis=1)
        kf = np.concatenate([k1_results[b * 4 + i]["k_o"] for i in range(4)],
                            axis=1)
        vf = np.concatenate([k1_results[b * 4 + i]["v_o"] for i in range(4)],
                            axis=1)
        for g in range(4):
            hs = slice(g * 512, (g + 1) * 512)
            vblocks = []
            for h in range(HPC):
                vp = vf[g * 512 + h * 128: g * 512 + (h + 1) * 128, :].T
                vb = np.ascontiguousarray(
                    vp.reshape(32, 128, 128).transpose(1, 0, 2).reshape(128, S))
                vblocks.append(vb)
            maps.append(dict(
                q_i=np.ascontiguousarray(qf[hs, :]),
                k_i=np.ascontiguousarray(kf[hs, :]),
                v_i=np.ascontiguousarray(np.concatenate(vblocks, axis=1)),
                rc=cos, rs=sin, rcq=cq, rsq=sq, cmask=cmask, ident=ident))
    return maps


def build_k3(phases="ABCD"):
    nc = bass.Bass("TRN2", target_bir_lowering=False)
    T = TPC
    s_in = nc.dram_tensor("s_in", [E, T], BF16, kind="ExternalInput")
    x_fm = nc.dram_tensor("x_fm", [E, T], F32, kind="ExternalInput")
    g0_in = nc.dram_tensor("g0_in", [E, T], BF16, kind="ExternalInput")
    wo = nc.dram_tensor("wo", [E, E], F32, kind="ExternalInput")
    wq1 = nc.dram_tensor("wq1", [E, E], F32, kind="ExternalInput")
    wk1 = nc.dram_tensor("wk1", [E, E], F32, kind="ExternalInput")
    wv1 = nc.dram_tensor("wv1", [E, E], F32, kind="ExternalInput")
    wb1 = nc.dram_tensor("wb1", [E, 16], F32, kind="ExternalInput")
    wfa = nc.dram_tensor("wfa", [E, 128], F32, kind="ExternalInput")
    wfb = nc.dram_tensor("wfb", [128, E], F32, kind="ExternalInput")
    wga = nc.dram_tensor("wga", [E, 128], F32, kind="ExternalInput")
    wgb = nc.dram_tensor("wgb", [128, E], F32, kind="ExternalInput")
    alog = nc.dram_tensor("alog", [1, 16], F32, kind="ExternalInput")
    dtb = nc.dram_tensor("dtb", [128, 16], F32, kind="ExternalInput")
    mgw = nc.dram_tensor("mgw", [E, 8], F32, kind="ExternalInput")
    wge = nc.dram_tensor("wge", [8 * E, 1408], F32, kind="ExternalInput")
    wue = nc.dram_tensor("wue", [8 * E, 1408], F32, kind="ExternalInput")
    wde = nc.dram_tensor("wde", [8 * 1408, E], F32, kind="ExternalInput")
    swg = nc.dram_tensor("swg", [E, 2816], F32, kind="ExternalInput")
    swu = nc.dram_tensor("swu", [E, 2816], F32, kind="ExternalInput")
    swd = nc.dram_tensor("swd", [2816, E], F32, kind="ExternalInput")

    x1_o = nc.dram_tensor("x1_o", [E, T], F32, kind="ExternalOutput")
    q_o = nc.dram_tensor("q_o", [T, E], BF16, kind="ExternalOutput")
    k_o = nc.dram_tensor("k_o", [T, E], BF16, kind="ExternalOutput")
    v_o = nc.dram_tensor("v_o", [T, E], BF16, kind="ExternalOutput")
    gate_o = nc.dram_tensor("gate_o", [E, T], BF16, kind="ExternalOutput")
    g_o = nc.dram_tensor("g_o", [E, T], F32, kind="ExternalOutput")
    beta_o = nc.dram_tensor("beta_o", [16, T], BF16, kind="ExternalOutput")
    gs1_o = nc.dram_tensor("gs1_o", [E, T], BF16, kind="ExternalOutput")

    with ExitStack() as stack:
        c = Ctx(nc, stack)
        pb = [c.ps(f"pb{i}", [128, 512]) for i in range(8)]
        bank_guard = [("", 0)] * 8

        def bank_wait(i):
            c.gwait(nc.tensor, bank_guard[i])

        xb = c.sb("xb", [128, 16 * T], BF16)
        ones_c = c.sb("ones_c", [128, 1], BF16)
        ones_r = c.sb("ones_r", [1, 128], BF16)
        c.constcol(1e-5, 1)
        c.constcol(1e-6, 128)
        c.constcol(1.0, 128)
        c.v("memset", ones_c[:, :], 1.0)
        c.v("memset", ones_r[:, :], 1.0)

        stAB = ExitStack()
        cAB = Ctx(nc, stAB)
        cAB._sems, cAB.cnt = c._sems, c.cnt
        scr = cAB.sb("scr", [1, 4 * T], F32)
        row_bf = cAB.sb("row_bf", [1, 2 * T], BF16)
        MU = cAB.sb("MU", [128, T], BF16)
        RSTD = cAB.sb("RSTD", [128, T], BF16)

        x1 = cAB.sb("x1", [128, 16 * T], F32)
        with ExitStack() as stA:
            cA = Ctx(nc, stA)
            cA._sems, cA.cnt = c._sems, c.cnt
            sfm = cA.sb("sfm", [128, 16 * T], BF16)
            g0t = cA.sb("g0t", [128, 2 * T], BF16)
            xint = cA.sb("xint", [128, 2 * T], F32)
            wbuf = cA.sb("wbuf", [128, 2 * 2048], BF16)
            tmpf = cA.sb("tmpf", [128, 2 * 512], F32)
            c.dma(nc.sync, sfm[:, :], fm_slab(s_in, slice(0, E), slice(0, T)))
            par_guard = [("", 0), ("", 0)]
            xg_guard = [("", 0), ("", 0)]
            rr = [0]
            for ct in range(16):
                p = ct % 2
                c.gwait(nc.gpsimd, par_guard[p])
                c.dma(nc.gpsimd, wbuf[:, p * 2048:(p + 1) * 2048],
                      fm_slab(wo, slice(0, E), slice(ct * 128, (ct + 1) * 128)))
                c.gwait(nc.sync, xg_guard[p])
                c.dma(nc.sync, g0t[:, p * T:(p + 1) * T],
                      g0_in[ct * 128:(ct + 1) * 128, :])
                c.dma(nc.sync, xint[:, p * T:(p + 1) * T],
                      x_fm[ct * 128:(ct + 1) * 128, :])
                ldc = c.ldmark()
                for th in range(2):
                    bi = rr[0]
                    rr[0] = (rr[0] + 1) % 8
                    bank_wait(bi)
                    c.ldwait(nc.tensor, ldc)
                    for dc in range(16):
                        m = c.mmq(
                            pb[bi][:, :],
                            wbuf[:, p * 2048 + dc * 128:p * 2048 + (dc + 1) * 128],
                            sfm[:, dc * T + th * 512:dc * T + (th + 1) * 512],
                            start=(dc == 0), stop=(dc == 15))
                    c.inc(m, "mm")
                    c.wait(nc.vector, "mm")
                    c.ldwait(nc.vector, ldc)
                    sl = slice(ct * T + th * 512, ct * T + (th + 1) * 512)
                    psl = slice(p * T + th * 512, p * T + (th + 1) * 512)
                    c.v("scalar_tensor_tensor", tmpf[:, th * 512:(th + 1) * 512],
                        pb[bi][:, :], 1.0, g0t[:, psl], ALU.mult, ALU.mult)
                    c.v("tensor_add", x1[:, sl], tmpf[:, th * 512:(th + 1) * 512],
                        xint[:, psl])
                    bank_guard[bi] = ("v", c.cnt["v"])
                par_guard[p] = ("mm", c.cnt["mm"])
                xg_guard[p] = ("v", c.cnt["v"])
            c.wait(nc.sync, "v")
            c.dma(nc.sync, x1_o[:, :].rearrange("(c p) n -> p c n", p=128),
                  x1[:, :], name="st")
            c.fence()

        if "B" not in phases:
            c.stwait(nc.sync)
            stAB.close()
            return nc
        with ExitStack() as stB:
            cB = Ctx(nc, stB)
            cB._sems, cB.cnt = c._sems, c.cnt
            sq = cB.sb("sq", [128, 2 * T], BF16)
            tmpf = cB.sb("tmpf", [128, T], F32)
            for i in range(8):
                bank_wait(i)
            for ci in range(16):
                xc = x1[:, ci * T:(ci + 1) * T]
                xbc = xb[:, ci * T:(ci + 1) * T]
                sqc = sq[:, (ci % 2) * T:(ci % 2) * T + T]
                if ci >= 2:
                    c.wait(nc.scalar, "mm")
                c.v("tensor_copy", xbc, xc)
                c.wait(nc.scalar, "v")
                c.a("activation", sqc, xbc, AF.Square)
                c.wait(nc.tensor, "v")
                c.wait(nc.tensor, "a")
                for th in range(2):
                    c.mmq(pb[th][0:1, :], ones_c[:, :],
                          xbc[:, th * 512:(th + 1) * 512],
                          start=(ci == 0), stop=(ci == 15))
                    m = c.mmq(pb[2 + th][0:1, :], ones_c[:, :],
                              sqc[:, th * 512:(th + 1) * 512],
                              start=(ci == 0), stop=(ci == 15))
                c.inc(m, "mm")
            s1 = scr[0:1, 0:T]
            s2 = scr[0:1, T:2 * T]
            mu = scr[0:1, 2 * T:3 * T]
            var = scr[0:1, 3 * T:4 * T]
            c.wait(nc.vector, "mm")
            for th in range(2):
                c.v("tensor_scalar", mu[:, th * 512:(th + 1) * 512],
                    pb[th][0:1, :], 1.0 / E, None, ALU.mult)
                c.v("tensor_scalar", s2[:, th * 512:(th + 1) * 512],
                    pb[2 + th][0:1, :], 1.0 / E, None, ALU.mult)
            c.v("tensor_mul", s1[:, :], mu[:, :], mu[:, :])
            c.v("tensor_sub", var[:, :], s2[:, :], s1[:, :])
            c.wait(nc.scalar, "v")
            c.a("activation", var[:, :], var[:, :], AF.Ln,
                bias=c.constcol(1e-5, 1))
            c.a("activation", var[:, :], var[:, :], AF.Exp, scale=-0.5)
            mu_bf = row_bf[0:1, 0:T]
            rstd_bf = row_bf[0:1, T:2 * T]
            c.wait(nc.vector, "a")
            c.v("tensor_copy", mu_bf, mu[:, :])
            c.v("tensor_copy", rstd_bf, var[:, :])
            c.wait(nc.tensor, "v")
            for th in range(2):
                c.mmq(pb[4 + th][:, :], ones_r[:, :],
                      mu_bf[:, th * 512:(th + 1) * 512], start=True, stop=True)
                m = c.mmq(pb[6 + th][:, :], ones_r[:, :],
                          rstd_bf[:, th * 512:(th + 1) * 512], start=True,
                          stop=True)
            c.inc(m, "mm")
            c.wait(nc.vector, "mm")
            c.v("tensor_copy", MU[:, 0:512], pb[4][:, :])
            c.v("tensor_copy", MU[:, 512:1024], pb[5][:, :])
            c.v("tensor_copy", RSTD[:, 0:512], pb[6][:, :])
            c.v("tensor_copy", RSTD[:, 512:1024], pb[7][:, :])
            for ci in range(16):
                xc = x1[:, ci * T:(ci + 1) * T]
                xbc = xb[:, ci * T:(ci + 1) * T]
                c.v("tensor_sub", tmpf[:, :], xc, MU[:, :])
                c.v("tensor_mul", xbc, tmpf[:, :], RSTD[:, :])
            for i in range(8):
                bank_guard[i] = ("v", c.cnt["v"])
        c.fence()
        stAB.close()
        if "C" not in phases:
            return nc
        return _k3_tail(nc, c, stack, pb, bank_guard, bank_wait, xb, ones_c,
                        ones_r, phases,
                        dict(wq1=wq1, wk1=wk1, wv1=wv1, wb1=wb1, wfa=wfa,
                             wfb=wfb, wga=wga, wgb=wgb, alog=alog, dtb=dtb,
                             mgw=mgw, wge=wge, wue=wue, wde=wde, swg=swg,
                             swu=swu, swd=swd),
                        dict(q_o=q_o, k_o=k_o, v_o=v_o, gate_o=gate_o,
                             g_o=g_o, beta_o=beta_o, gs1_o=gs1_o))


def _k3_tail(nc, c, stack, pb, bank_guard, bank_wait, xb, ones_c, ones_r,
             phases, W, O):
    T = TPC
    rr = [0]

    def acq():
        i = rr[0]
        rr[0] = (rr[0] + 1) % 8
        bank_wait(i)
        return i

    def seg_bcast_ap(t, nseg, seglen):
        return bass.AP(t, 0, [[nseg, 128], [1, nseg], [0, seglen]])

    with ExitStack() as stC:
        cC = Ctx(nc, stC)
        cC._sems, cC.cnt = c._sems, c.cnt
        wslab = cC.sb("wslab", [128, 2 * 16 * 512], BF16)
        ttm = cC.sb("ttm", [128, 16 * 128], BF16)
        sqf = cC.sb("sqf", [128, E], F32)
        ssum = cC.sb("ssum", [128, 16], F32)
        small = cC.sb("small", [128, 4 * T], BF16)
        deccol = cC.sb("deccol", [128, 16], F32)
        alb = cC.sb("alb", [128, 16], F32)
        arow = cC.sb("arow", [1, 16], F32)
        ones_rf = cC.sb("ones_rf", [1, 128], F32)
        c.v("memset", ones_rf[:, :], 1.0)
        zf = cC.sb("zf", [128, 512], F32)
        spf = cC.sb("spf", [128, 3 * 512], F32)
        gstg = cC.sb("gstg", [128, T], F32)
        c.dma(nc.sync, deccol[:, :], W["dtb"][:, :])
        c.dma(nc.sync, arow[:, :], W["alog"][:, :])
        c.ldwait(nc.scalar)
        c.a("activation", arow[:, :], arow[:, :], AF.Exp)
        c.wait(nc.tensor, "a")
        c.wait(nc.tensor, "v")
        m = c.mmq(pb[acq()][:, 0:16], ones_rf[:, :], arow[:, :],
                  start=True, stop=True)
        c.inc(m, "mm")
        c.wait(nc.vector, "mm")
        c.v("tensor_copy", alb[:, :], pb[(rr[0] + 7) % 8][:, 0:16])
        bank_guard[(rr[0] + 7) % 8] = ("v", c.cnt["v"])

        ttm_all = cC.sb("ttm_all", [128, 8 * E], BF16)
        par_guard = [("", 0), ("", 0)]
        for wi, (w_in, w_out, do_l2) in enumerate(
                [(W["wq1"], O["q_o"], True), (W["wk1"], O["k_o"], True),
                 (W["wv1"], O["v_o"], False)]):
            c.stwait(nc.vector)
            for cg in range(4):
                p = cg % 2
                c.gwait(nc.gpsimd, par_guard[p])
                c.dma(nc.gpsimd, wslab[:, p * 8192:(p + 1) * 8192],
                      fm_slab(w_in, slice(0, E),
                              slice(cg * 512, (cg + 1) * 512)))
                ldc = c.ldmark()
                for tt in range(8):
                    bi = acq()
                    c.ldwait(nc.tensor, ldc)
                    for dc in range(16):
                        m = c.mmq(pb[bi][:, :],
                                  xb[:, dc * T + tt * 128:dc * T + (tt + 1) * 128],
                                  wslab[:, p * 8192 + dc * 512:
                                        p * 8192 + (dc + 1) * 512],
                                  start=(dc == 0), stop=(dc == 15))
                    c.inc(m, "mm")
                    c.wait(nc.vector, "mm")
                    c.v("tensor_copy",
                        ttm_all[:, tt * E + cg * 512:tt * E + (cg + 1) * 512],
                        pb[bi][:, :])
                    bank_guard[bi] = ("v", c.cnt["v"])
                par_guard[p] = ("mm", c.cnt["mm"])
            for tt in range(8):
                tsl = slice(tt * 128, (tt + 1) * 128)
                trow = ttm_all[:, tt * E:(tt + 1) * E]
                if do_l2:
                    c.v("tensor_mul", sqf[:, :], trow, trow)
                    c.v("tensor_reduce", ssum[:, :],
                        bass.AP(sqf, 0, [[E, 128], [128, 16], [1, 128]]),
                        AX.X, ALU.add)
                    c.wait(nc.scalar, "v")
                    c.a("activation", ssum[:, :], ssum[:, :], AF.Ln,
                        bias=c.constcol(1e-6, 128))
                    c.a("activation", ssum[:, :], ssum[:, :], AF.Exp,
                        scale=-0.5)
                    c.wait(nc.vector, "a")
                    c.v("tensor_mul", trow, trow,
                        seg_bcast_ap(ssum, 16, 128))
                c.wait(nc.sync, "v")
                c.dma(nc.sync, w_out[tsl, :], trow, name="st")

        wbq = wslab[:, 0:16 * 16]
        c.stwait(nc.gpsimd)
        c.dma(nc.gpsimd, wbq, fm_slab(W["wb1"], slice(0, E), slice(0, 16)))
        ldc = c.ldmark()
        for th in range(2):
            bi = acq()
            c.ldwait(nc.tensor, ldc)
            for dc in range(16):
                m = c.mmq(pb[bi][0:16, :], wbq[:, dc * 16:(dc + 1) * 16],
                          xb[:, dc * T + th * 512:dc * T + (th + 1) * 512],
                          start=(dc == 0), stop=(dc == 15))
            c.inc(m, "mm")
            c.wait(nc.scalar, "mm")
            c.a("activation", ttm[0:16, th * 512:(th + 1) * 512],
                pb[bi][0:16, :], AF.Sigmoid)
            bank_guard[bi] = ("a", c.cnt["a"])
        c.wait(nc.sync, "a")
        c.dma(nc.sync, O["beta_o"][:, :], ttm[0:16, 0:T], name="st")

        for li, (wa, wbmat, outk) in enumerate(
                [(W["wfa"], W["wfb"], "g"), (W["wga"], W["wgb"], "gate")]):
            c.stwait(nc.gpsimd)
            c.wait(nc.gpsimd, "mm")
            c.dma(nc.gpsimd, wslab[:, 0:2048],
                  fm_slab(wa, slice(0, E), slice(0, 128)))
            ldc = c.ldmark()
            t1 = small[:, li * T:(li + 1) * T]
            for th in range(2):
                bi = acq()
                c.ldwait(nc.tensor, ldc)
                for dc in range(16):
                    m = c.mmq(pb[bi][:, :],
                              wslab[:, dc * 128:(dc + 1) * 128],
                              xb[:, dc * T + th * 512:dc * T + (th + 1) * 512],
                              start=(dc == 0), stop=(dc == 15))
                c.inc(m, "mm")
                c.wait(nc.vector, "mm")
                c.v("tensor_copy", t1[:, th * 512:(th + 1) * 512], pb[bi][:, :])
                bank_guard[bi] = ("v", c.cnt["v"])
            c.wait(nc.gpsimd, "mm")
            c.dma(nc.gpsimd, wslab[:, 8192:8192 + 16 * 128],
                  fm_slab(wbmat, slice(0, 128), slice(0, E), p=128))
            ldc = c.ldmark()
            for ct in range(16):
                for th in range(2):
                    bi = acq()
                    c.ldwait(nc.tensor, ldc)
                    c.wait(nc.tensor, "v")
                    m = c.mmq(pb[bi][:, :],
                              wslab[:, 8192 + ct * 128:8192 + (ct + 1) * 128],
                              t1[:, th * 512:(th + 1) * 512],
                              start=True, stop=True)
                    c.inc(m, "mm")
                    if outk == "gate":
                        c.wait(nc.vector, "mm")
                        c.stwait(nc.vector)
                        c.v("tensor_copy",
                            small[:, 2 * T + th * 512:2 * T + (th + 1) * 512],
                            pb[bi][:, :])
                        bank_guard[bi] = ("v", c.cnt["v"])
                        if th == 1:
                            c.wait(nc.sync, "v")
                            c.dma(nc.sync,
                                  O["gate_o"][ct * 128:(ct + 1) * 128, :],
                                  small[:, 2 * T:3 * T], name="st")
                    else:
                        c.wait(nc.vector, "mm")
                        c.v("tensor_scalar", zf[:, :], pb[bi][:, :],
                            deccol[:, ct:ct + 1], None, ALU.add)
                        bank_guard[bi] = ("v", c.cnt["v"])
                        c.wait(nc.scalar, "v")
                        c.a("activation", spf[:, 0:512], zf[:, :], AF.Abs)
                        c.a("activation", spf[:, 0:512], spf[:, 0:512],
                            AF.Exp, scale=-1.0)
                        c.a("activation", spf[:, 512:1024], spf[:, 0:512],
                            AF.Ln, bias=c.constcol(1.0, 128))
                        c.wait(nc.vector, "a")
                        c.v("tensor_scalar_max", spf[:, 1024:1536], zf[:, :],
                            0.0)
                        c.v("tensor_add", spf[:, 1024:1536],
                            spf[:, 1024:1536], spf[:, 512:1024])
                        c.v("tensor_scalar",
                            gstg[:, th * 512:(th + 1) * 512],
                            spf[:, 1024:1536], alb[:, ct:ct + 1], -1.0,
                            ALU.mult, ALU.mult)
                        if th == 1:
                            c.wait(nc.sync, "v")
                            c.dma(nc.sync, O["g_o"][ct * 128:(ct + 1) * 128, :],
                                  gstg[:, :], name="st")
                            c.stwait(nc.vector)
        c.fence()

    if "D" not in phases:
        return nc
    import os
    dsub = int(os.environ.get("MK_DSUB", "99"))
    with ExitStack() as stD:
        cD = Ctx(nc, stD)
        cD._sems, cD.cnt = c._sems, c.cnt
        acc = cD.sb("acc", [128, 16 * T], F32)
        h_e = cD.sb("h_e", [128, 22 * T], BF16)
        wsl = cD.sb("wsl", [128, 2 * 2048], BF16)
        wdl = cD.sb("wdl", [128, 2 * 2048], BF16)
        gact = cD.sb("gact", [128, 2 * 512], BF16)
        CB = cD.sb("CB", [128, T], BF16)
        lg = cD.sb("lg", [8, T], F32)
        ptm = cD.sb("ptm", [128, 8 * 8], F32)
        comb = cD.sb("comb", [8, T], BF16)
        idf = cD.sb("idf", [128, 128], F32)
        m1 = cD.sb("m1", [128, 4], F32)
        msk = cD.sb("msk", [128, 16], F32)
        stg = cD.sb("stg", [128, 2 * 512], BF16)
        iot = cD.sb("iot", [128, 128], I32)
        idx = cD.sb("idx", [128, 128], I32)
        sel8 = cD.sb("sel8", [8, 8 * 128], BF16)
        i8k = cD.sb("i8k", [8, 8 * 128], I32)
        i8e = cD.sb("i8e", [8, 8 * 128], I32)
        c.inc(nc.gpsimd.iota(iot[:, :], [[0, 128]], channel_multiplier=1), "p")
        c.inc(nc.gpsimd.iota(idx[:, :], [[1, 128]], channel_multiplier=0), "p")
        c.inc(nc.gpsimd.iota(i8k[:, :], [[0, 8 * 128]], channel_multiplier=1),
              "p")
        c.inc(nc.gpsimd.iota(i8e[:, :], [[1, 8], [0, 128]],
                             channel_multiplier=0), "p")
        c.wait(nc.vector, "p")
        c.v("tensor_tensor", idf[:, :], iot[:, :], idx[:, :], ALU.is_equal)
        c.v("tensor_tensor", sel8[:, :], i8k[:, :], i8e[:, :], ALU.is_equal)
        c.stwait(nc.gpsimd)
        c.dma(nc.gpsimd, wsl[:, 0:16 * 8],
              fm_slab(W["mgw"], slice(0, E), slice(0, 8)))
        ldc = c.ldmark()
        for th in range(2):
            bi = acq()
            c.ldwait(nc.tensor, ldc)
            for dc in range(16):
                m = c.mmq(pb[bi][0:8, :], wsl[:, dc * 8:(dc + 1) * 8],
                          xb[:, dc * T + th * 512:dc * T + (th + 1) * 512],
                          start=(dc == 0), stop=(dc == 15))
            c.inc(m, "mm")
            c.wait(nc.vector, "mm")
            c.v("tensor_copy", lg[:, th * 512:(th + 1) * 512], pb[bi][0:8, :])
            bank_guard[bi] = ("v", c.cnt["v"])
        c.wait(nc.tensor, "v")
        for tc in range(8):
            bi = acq()
            m = nc.tensor.transpose(pb[bi][:, 0:8], lg[0:8, tc * 128:(tc + 1) * 128],
                                    idf[0:8, 0:8])
            c.inc(m, "mm")
            c.wait(nc.vector, "mm")
            pt = ptm[:, tc * 8:(tc + 1) * 8]
            c.v("tensor_copy", pt, pb[bi][:, 0:8])
            bank_guard[bi] = ("v", c.cnt["v"])
            c.v("tensor_reduce", m1[:, 0:1], pt, AX.X, ALU.max, negate=True)
            c.wait(nc.scalar, "v")
            c.a("activation", pt, pt, AF.Exp, bias=m1[:, 0:1])
            c.wait(nc.vector, "a")
            c.v("tensor_reduce", m1[:, 1:2], pt, AX.X, ALU.add)
            c.v("reciprocal", m1[:, 2:3], m1[:, 1:2])
            c.v("tensor_scalar", pt, pt, m1[:, 2:3], None, ALU.mult)
            c.v("tensor_reduce", m1[:, 0:1], pt, AX.X, ALU.max)
            c.v("tensor_scalar", msk[:, 0:8], pt, m1[:, 0:1], None,
                ALU.is_equal)
            c.v("scalar_tensor_tensor", msk[:, 8:16], msk[:, 0:8], -1e30,
                pt, ALU.mult, ALU.add)
            c.v("tensor_reduce", m1[:, 1:2], msk[:, 8:16], AX.X, ALU.max)
            c.v("tensor_scalar", msk[:, 8:16], msk[:, 8:16], m1[:, 1:2], None,
                ALU.is_equal)
            c.v("tensor_add", msk[:, 0:8], msk[:, 0:8], msk[:, 8:16])
            c.v("tensor_mul", pt, pt, msk[:, 0:8])
        c.wait(nc.tensor, "v")
        for tc in range(8):
            bi = acq()
            m = nc.tensor.transpose(pb[bi][0:8, 0:128],
                                    ptm[:, tc * 8:(tc + 1) * 8], idf[:, :])
            c.inc(m, "mm")
            c.wait(nc.vector, "mm")
            c.v("tensor_copy", comb[:, tc * 128:(tc + 1) * 128],
                pb[bi][0:8, 0:128])
            bank_guard[bi] = ("v", c.cnt["v"])

        par_guard = [("mm", c.cnt["mm"]), ("mm", c.cnt["mm"])]
        dpar_guard = [("", 0), ("", 0)]
        for e in range(min(9, dsub)):
            shared = (e == 8)
            NIC = 22 if shared else 11
            wg_d = W["swg"] if shared else W["wge"]
            wu_d = W["swu"] if shared else W["wue"]
            wd_d = W["swd"] if shared else W["wde"]
            gr0 = 0 if shared else e * E
            dr0 = 0 if shared else e * 1408
            if not shared:
                c.wait(nc.tensor, "v")
                for th in range(2):
                    bi = acq()
                    m = c.mmq(pb[bi][:, :],
                              sel8[:, e * 128:(e + 1) * 128],
                              comb[0:8, th * 512:(th + 1) * 512],
                              start=True, stop=True)
                    c.inc(m, "mm")
                    c.wait(nc.vector, "mm")
                    c.v("tensor_copy", CB[:, th * 512:(th + 1) * 512],
                        pb[bi][:, :])
                    bank_guard[bi] = ("v", c.cnt["v"])
            dsub2 = int(os.environ.get("MK_DSUB2", "99"))
            if dsub2 < 2:
                continue
            for ic in range(NIC):
                p = ic % 2
                c.gwait(nc.gpsimd, par_guard[p])
                c.dma(nc.gpsimd, wsl[:, p * 2048:(p + 1) * 2048],
                      fm_slab(wg_d, slice(gr0, gr0 + E),
                              slice(ic * 128, (ic + 1) * 128)))
                c.dma(nc.gpsimd, wdl[:, p * 2048:(p + 1) * 2048],
                      fm_slab(wu_d, slice(gr0, gr0 + E),
                              slice(ic * 128, (ic + 1) * 128)))
                ldc = c.ldmark()
                for th in range(2):
                    bg = acq()
                    bu = acq()
                    c.ldwait(nc.tensor, ldc)
                    for dc in range(16):
                        c.mmq(pb[bg][:, :],
                              wsl[:, p * 2048 + dc * 128:p * 2048 + (dc + 1) * 128],
                              xb[:, dc * T + th * 512:dc * T + (th + 1) * 512],
                              start=(dc == 0), stop=(dc == 15))
                    for dc in range(16):
                        m = c.mmq(pb[bu][:, :],
                                  wdl[:, p * 2048 + dc * 128:
                                      p * 2048 + (dc + 1) * 128],
                                  xb[:, dc * T + th * 512:dc * T + (th + 1) * 512],
                                  start=(dc == 0), stop=(dc == 15))
                    c.inc(m, "mm")
                    c.wait(nc.scalar, "mm")
                    c.wait(nc.scalar, "v")
                    c.a("activation", gact[:, th * 512:(th + 1) * 512],
                        pb[bg][:, :], AF.Silu)
                    c.wait(nc.vector, "a")
                    hop = h_e[:, ic * T + th * 512:ic * T + (th + 1) * 512]
                    c.v("scalar_tensor_tensor", hop, pb[bu][:, :], 1.0,
                        gact[:, th * 512:(th + 1) * 512], ALU.mult, ALU.mult)
                    if not shared:
                        c.v("tensor_mul", hop, hop,
                            CB[:, th * 512:(th + 1) * 512])
                    bank_guard[bg] = ("a", c.cnt["a"])
                    bank_guard[bu] = ("v", c.cnt["v"])
                par_guard[p] = ("mm", c.cnt["mm"])
            dpar_guard[0] = ("mm", c.cnt["mm"])
            dpar_guard[1] = ("mm", c.cnt["mm"])
            if dsub2 < 3:
                continue
            for th in range(2):
                for dh in range(2):
                    for i in range(8):
                        bank_wait(i)
                    for ic in range(NIC):
                        p = ic % 2
                        c.gwait(nc.gpsimd, dpar_guard[p])
                        c.dma(nc.gpsimd, wdl[:, p * 1024:(p + 1) * 1024],
                              wd_d[dr0 + ic * 128:dr0 + (ic + 1) * 128,
                                   dh * 1024:(dh + 1) * 1024])
                        c.ldwait(nc.tensor)
                        c.wait(nc.tensor, "v")
                        for dt in range(8):
                            m = c.mmq(
                                pb[dt][:, :],
                                wdl[:, p * 1024 + dt * 128:
                                    p * 1024 + (dt + 1) * 128],
                                h_e[:, ic * T + th * 512:ic * T + (th + 1) * 512],
                                start=(ic == 0), stop=(ic == NIC - 1))
                        c.inc(m, "mm")
                        dpar_guard[p] = ("mm", c.cnt["mm"])
                    c.wait(nc.vector, "mm")
                    for dt in range(8):
                        d_abs = dh * 8 + dt
                        asl = slice(d_abs * T + th * 512,
                                    d_abs * T + (th + 1) * 512)
                        if e == 0:
                            c.v("tensor_copy", acc[:, asl], pb[dt][:, :])
                        else:
                            c.v("tensor_add", acc[:, asl], acc[:, asl],
                                pb[dt][:, :])
                        bank_guard[dt] = ("v", c.cnt["v"])
            par_guard[0] = ("mm", c.cnt["mm"])
            par_guard[1] = ("mm", c.cnt["mm"])
        if dsub == 0:
            c.wait(nc.sync, "v")
            return nc
        for ct in range(16):
            for th in range(2):
                pg = th
                c.wait(nc.scalar, "v")
                c.stwait(nc.scalar)
                c.a("activation", stg[:, pg * 512:(pg + 1) * 512],
                    acc[:, ct * T + th * 512:ct * T + (th + 1) * 512], AF.Silu)
            c.wait(nc.sync, "a")
            c.dma(nc.sync, O["gs1_o"][ct * 128:(ct + 1) * 128, :],
                  stg[:, 0:T], name="st")
        c.stwait(nc.sync)
    return nc


def host_prep_k3(inputs, k1_results, k2_results):
    x = np.asarray(inputs["L0_attn_x"], dtype=np.float32)
    f32 = lambda k: np.asarray(inputs[k], np.float32)
    com = dict(
        wo=f32("L0_attn_wo"), wq1=f32("L1_attn_Wq"), wk1=f32("L1_attn_Wk"),
        wv1=f32("L1_attn_Wv"), wb1=f32("L1_attn_Wb"), wfa=f32("L1_attn_Wfa"),
        wfb=f32("L1_attn_Wfb"), wga=f32("L1_attn_Wga"), wgb=f32("L1_attn_Wgb"),
        alog=np.ascontiguousarray(f32("L1_attn_A_log").reshape(1, 16)),
        dtb=np.ascontiguousarray(f32("L1_attn_dt_bias").reshape(16, 128).T),
        mgw=f32("L1_ffn_gate_w"),
        wge=np.ascontiguousarray(f32("L1_ffn_w_gate").reshape(8 * E, 1408)),
        wue=np.ascontiguousarray(f32("L1_ffn_w_up").reshape(8 * E, 1408)),
        wde=np.ascontiguousarray(f32("L1_ffn_w_down").reshape(8 * 1408, E)),
        swg=f32("L1_ffn_sw_gate"), swu=f32("L1_ffn_sw_up"),
        swd=f32("L1_ffn_sw_down"))
    maps = []
    for ci in range(NCORE):
        b, qq = divmod(ci, 4)
        tsl = slice(qq * TPC, (qq + 1) * TPC)
        s_in = np.concatenate(
            [np.asarray(k2_results[b * 4 + g]["ao_o"])[:, tsl]
             for g in range(4)], axis=0)
        m = dict(com)
        m["s_in"] = np.ascontiguousarray(s_in)
        m["x_fm"] = np.ascontiguousarray(x[b, tsl, :].T)
        m["g0_in"] = np.asarray(k1_results[ci]["gs_o"])
        maps.append(m)
    return maps


LEVELS = (1, 2, 4, 8, 16, 32, 64)
SEG = 1024
NSEG = S // SEG
CPS = SEG // 128


def build_k4():
    nc = bass.Bass("TRN2", target_bir_lowering=False)
    q_i = nc.dram_tensor("q_i", [512, S], BF16, kind="ExternalInput")
    k_i = nc.dram_tensor("k_i", [512, S], BF16, kind="ExternalInput")
    v_i = nc.dram_tensor("v_i", [512, S], BF16, kind="ExternalInput")
    g_i = nc.dram_tensor("g_i", [512, S], F32, kind="ExternalInput")
    gate_i = nc.dram_tensor("gate_i", [S, 512], BF16, kind="ExternalInput")
    beta_r = nc.dram_tensor("beta_r", [4, S], BF16, kind="ExternalInput")
    beta_t2 = nc.dram_tensor("beta_t2", [4, 128, 32], BF16,
                             kind="ExternalInput")
    cvw = nc.dram_tensor("cvw", [128, 48], F32, kind="ExternalInput")
    rmsb = nc.dram_tensor("rmsb", [128, 128], F32, kind="ExternalInput")
    smsk = nc.dram_tensor("smsk", [128, 7 * SEG], BF16, kind="ExternalInput")
    dmsk = nc.dram_tensor("dmsk", [128, 7 * 128], BF16, kind="ExternalInput")
    ident = nc.dram_tensor("ident", [128, 128], BF16, kind="ExternalInput")
    out_o = nc.dram_tensor("out_o", [S, 512], BF16, kind="ExternalOutput")

    NL = len(LEVELS)
    with ExitStack() as stack:
        c = Ctx(nc, stack)
        pb = [c.ps(f"pb{i}", [128, 512]) for i in range(7)]
        tb = c.ps("tb", [128, 1024], BF16)
        bank_guard = [("", 0)] * 7
        tb_guard = [("", 0), ("", 0)]
        rrb = [0]

        def bank_wait(i):
            s_, v_ = bank_guard[i]
            if s_:
                c.wait(nc.tensor, s_, v_)

        def acq():
            i = rrb[0]
            rrb[0] = (rrb[0] + 1) % 7
            bank_wait(i)
            return i

        id_sb = c.sb("id_sb", [128, 128], BF16)
        rms_sb = c.sb("rms_sb", [128, 128], F32)
        cvw_sb = c.sb("cvw_sb", [128, 48], F32)
        mks = c.sb("mks", [128, 7 * SEG], BF16)
        mkd = c.sb("mkd", [128, 7 * 128], BF16)
        ones_r = c.sb("ones_r", [1, 128], BF16)
        ones_cc = c.sb("ones_cc", [128, 1], BF16)
        qkb = c.sb("qkb", [128, 128], BF16)
        c.v("memset", ones_r[:, :], 1.0)
        c.v("memset", ones_cc[:, :], 1.0)
        c.dma(nc.sync, id_sb[:, :], ident[:, :])
        c.dma(nc.sync, rms_sb[:, :], rmsb[:, :])
        c.dma(nc.sync, cvw_sb[:, :], cvw[:, :])
        c.dma(nc.sync, mks[:, :], smsk[:, :])
        c.dma(nc.sync, mkd[:, :], dmsk[:, :])

        T_f = c.sb("T_f", [128, 4 * 128], F32)
        T_b = c.sb("T_b", [128, 4 * 128], BF16)
        c.v("memset", T_f[:, :], 0.0)
        c.v("memset", T_b[:, :], 0.0)

        HL = SEG + 3
        qcv = c.sb("qcv", [128, SEG], BF16)
        kcv = c.sb("kcv", [128, SEG], BF16)
        vcv = c.sb("vcv", [128, SEG], BF16)
        kbe = c.sb("kbe", [128, SEG], BF16)
        RA = c.sb("RA", [128, NL * SEG], BF16)
        RG = c.sb("RG", [128, NL * SEG], BF16)
        CD = c.sb("CD", [128, NL * SEG], BF16)
        QT = c.sb("QT", [128, SEG], BF16)
        KR = c.sb("KR", [128, SEG], BF16)
        KF = c.sb("KF", [128, SEG], BF16)
        gam = c.sb("gam", [128, CPS], F32)
        BB = c.sb("BB", [128, SEG], BF16)
        brow = c.sb("brow", [1, SEG], BF16)
        raw = c.sb("raw", [128, 3 * HL], F32)
        rawb = c.sb("rawb", [128, 3 * HL], BF16)
        gseg = c.sb("gseg", [128, SEG], F32)
        cacc = c.sb("cacc", [128, SEG], F32)
        scn = c.sb("scn", [128, 2 * SEG], F32)
        sfx = c.sb("sfx", [128, SEG], F32)
        zt = c.sb("zt", [128, 128], BF16)
        zn = c.sb("zn", [128, 128], BF16)
        t1t = c.sb("t1t", [128, 128], BF16)
        Us = c.sb("Us", [128, 128], BF16)
        Rs = c.sb("Rs", [128, 128], BF16)
        Os = c.sb("Os", [128, 128], F32)
        LTs = c.sb("LTs", [128, 128], BF16)
        GTs = c.sb("GTs", [128, 128], BF16)
        vpm = c.sb("vpm", [128, 128], BF16)
        kfpm = c.sb("kfpm", [128, 128], BF16)
        qkrow = c.sb("qkrow", [1, 128], BF16)
        qkcol = c.sb("qkcol", [128, 1], F32)
        bcseg = c.sb("bcseg", [128, CPS], F32)
        msc = c.sb("msc", [128, 4], F32)
        tscr = c.sb("tscr", [128, 128], F32)
        gslab = c.sb("gslab", [128, CPS * 128], BF16)
        oslab = c.sb("oslab", [128, CPS * 128], BF16)
        c.constcol(1e-5, 128)

        for h in range(HPC):
            hs = slice(h * 128, (h + 1) * 128)
            for sg in range(NSEG):
                t0 = sg * SEG
                c.wait(nc.sync, "mm")
                c.wait(nc.sync, "v")
                c.wait(nc.sync, "a")
                if sg == 0:
                    c.v("memset", rawb[:, 0:3], 0.0)
                    c.v("memset", rawb[:, HL:HL + 3], 0.0)
                    c.v("memset", rawb[:, 2 * HL:2 * HL + 3], 0.0)
                    c.wait(nc.sync, "v")
                    for ti, tt in enumerate((q_i, k_i, v_i)):
                        c.dma(nc.sync, rawb[:, ti * HL + 3:(ti + 1) * HL],
                              tt[hs, 0:SEG])
                else:
                    for ti, tt in enumerate((q_i, k_i, v_i)):
                        c.dma(nc.sync, rawb[:, ti * HL:(ti + 1) * HL],
                              tt[hs, t0 - 3:t0 + SEG])
                c.dma(nc.sync, gseg[:, :], g_i[hs, t0:t0 + SEG])
                c.dma(nc.sync, brow[:, :],
                      bass.AP(beta_r, h * S + t0, [[0, 1], [1, SEG]]))
                c.wait(nc.gpsimd, "v")
                c.dma(nc.gpsimd, bcseg[:, :],
                      bass.AP(beta_t2, h * 128 * 32 + sg * CPS,
                              [[32, 128], [1, CPS]]))
                c.dma(nc.sync, gslab[:, :],
                      gate_i[t0:t0 + SEG, hs].rearrange("(c p) n -> p c n",
                                                        p=128))
                ldm = c.ldmark()
                c.ldwait(nc.vector, ldm)
                c.ldwait(nc.scalar, ldm)
                for ti, dst in enumerate((qcv, kcv, vcv)):
                    src = rawb[:, ti * HL:(ti + 1) * HL]
                    wc = cvw_sb[:, ti * 16 + h * 4:ti * 16 + h * 4 + 4]
                    c.wait(nc.vector, "a")
                    c.v("tensor_scalar", cacc[:, :], src[:, 0:SEG],
                        wc[:, 0:1], None, ALU.mult)
                    for j in range(1, 4):
                        c.v("scalar_tensor_tensor", cacc[:, :],
                            src[:, j:SEG + j], wc[:, j:j + 1], cacc[:, :],
                            ALU.mult, ALU.add)
                    c.wait(nc.scalar, "v")
                    c.a("activation", dst[:, :], cacc[:, :], AF.Silu)
                c.wait(nc.tensor, "a")
                c.ldwait(nc.tensor, ldm)
                c.wait(nc.tensor, "v")
                bi0 = acq()
                bi1 = acq()
                c.mmq(pb[bi0][:, :], ones_r[:, :], brow[:, 0:512],
                      start=True, stop=True)
                m = c.mmq(pb[bi1][:, :], ones_r[:, :], brow[:, 512:1024],
                          start=True, stop=True)
                c.inc(m, "mm")
                c.wait(nc.vector, "mm")
                c.v("tensor_copy", BB[:, 0:512], pb[bi0][:, :])
                c.v("tensor_copy", BB[:, 512:1024], pb[bi1][:, :])
                bank_guard[bi0] = ("v", c.cnt["v"])
                bank_guard[bi1] = ("v", c.cnt["v"])
                c.wait(nc.vector, "a")
                c.v("tensor_mul", kbe[:, :], kcv[:, :], BB[:, :])
                c.wait(nc.scalar, "v")
                c.a("activation", RA[:, 0:SEG], gseg[:, :], AF.Exp)
                for li, lh in enumerate(LEVELS[1:] + (128,)):
                    mi = li
                    pp = li % 2
                    c.wait(nc.vector, "a")
                    c.v("tensor_tensor_scan", scn[:, pp * SEG:(pp + 1) * SEG],
                        mks[:, mi * SEG:(mi + 1) * SEG], gseg[:, :], 0.0,
                        ALU.mult, ALU.add)
                    fw = scn[:, pp * SEG:(pp + 1) * SEG]
                    ends = bass.AP(scn, pp * SEG + lh - 1,
                                   [[2 * SEG, 128], [lh, SEG // lh], [0, lh]])
                    c.v("tensor_tensor", sfx[:, :], ends, fw, ALU.subtract_rev
                        if False else ALU.subtract)
                    c.wait(nc.scalar, "v")
                    if lh != 128:
                        lvl = li + 1
                        c.a("activation", RG[:, lvl * SEG:(lvl + 1) * SEG],
                            fw, AF.Exp)
                        c.a("activation", CD[:, lvl * SEG:(lvl + 1) * SEG],
                            sfx[:, :], AF.Exp)
                    else:
                        c.a("activation", QT[:, :], fw, AF.Exp)
                        c.a("activation", KF[:, :], sfx[:, :], AF.Exp)
                        gend = bass.AP(scn, pp * SEG + 127,
                                       [[2 * SEG, 128], [128, CPS]])
                        c.a("activation", gam[:, :], gend, AF.Exp)
                c.wait(nc.vector, "a")
                c.v("tensor_copy", tscr[:, 0:SEG] if False else RG[:, 0:SEG],
                    RA[:, 0:SEG])
                c.v("tensor_mul", RA[:, 0:SEG], RA[:, 0:SEG], kbe[:, :])
                c.v("scalar_tensor_tensor", RG[:, 0:SEG], RG[:, 0:SEG],
                    D ** -0.5, qcv[:, :], ALU.mult, ALU.mult)
                c.v("tensor_copy", CD[:, 0:SEG], kcv[:, :])
                for lvl in range(1, NL):
                    sl = slice(lvl * SEG, (lvl + 1) * SEG)
                    c.v("tensor_mul", RA[:, sl], RG[:, sl], kbe[:, :])
                    c.v("scalar_tensor_tensor", RG[:, sl], RG[:, sl],
                        D ** -0.5, qcv[:, :], ALU.mult, ALU.mult)
                    c.v("tensor_mul", CD[:, sl], CD[:, sl], kcv[:, :])
                c.v("tensor_mul", KR[:, :], kbe[:, :], QT[:, :])
                c.v("scalar_tensor_tensor", QT[:, :], QT[:, :], D ** -0.5,
                    qcv[:, :], ALU.mult, ALU.mult)
                c.v("tensor_mul", KF[:, :], KF[:, :], kcv[:, :])
                c.wait(nc.tensor, "v")

                for ch in range(CPS):
                    cs = slice(ch * 128, (ch + 1) * 128)
                    ct0 = t0 + ch * 128
                    Tf = T_f[:, h * 128:(h + 1) * 128]
                    Tb = T_b[:, h * 128:(h + 1) * 128]
                    tg = ch % 2
                    c.gwait(nc.tensor, tb_guard[tg])
                    m = nc.tensor.transpose(tb[:, tg * 512:tg * 512 + 128],
                                            vcv[:, cs], id_sb[:, :])
                    c.inc(m, "mm")
                    bL = acq()
                    bL2 = acq()
                    for lvl in range(NL):
                        dst = pb[bL] if lvl < 4 else pb[bL2]
                        off = (lvl % 4) * 128
                        c.mmq(dst[:, off:off + 128], CD[:, lvl * SEG + ch * 128:
                                                        lvl * SEG + ch * 128 + 128],
                              RA[:, lvl * SEG + ch * 128:
                                 lvl * SEG + ch * 128 + 128],
                              start=(lvl % 4 == 0),
                              stop=(lvl % 4 == 3 or lvl == NL - 1))
                    bG = acq()
                    bG2 = acq()
                    for lvl in range(NL):
                        dst = pb[bG] if lvl < 4 else pb[bG2]
                        off = (lvl % 4) * 128
                        m = c.mmq(dst[:, off:off + 128],
                                  CD[:, lvl * SEG + ch * 128:
                                     lvl * SEG + ch * 128 + 128],
                                  RG[:, lvl * SEG + ch * 128:
                                     lvl * SEG + ch * 128 + 128],
                                  start=(lvl % 4 == 0),
                                  stop=(lvl % 4 == 3 or lvl == NL - 1))
                    c.inc(m, "mm")
                    c.wait(nc.vector, "mm")
                    c.v("tensor_copy", vpm[:, :], tb[:, tg * 512:tg * 512 + 128])
                    tb_guard[tg] = ("v", c.cnt["v"])
                    for lvl in range(NL):
                        src = pb[bL] if lvl < 4 else pb[bL2]
                        off = (lvl % 4) * 128
                        mk = mkd[:, lvl * 128:(lvl + 1) * 128]
                        if lvl == 0:
                            c.v("tensor_tensor", LTs[:, :], src[:, off:off + 128],
                                mk, ALU.mult)
                        else:
                            c.v("tensor_tensor", tscr[:, :],
                                src[:, off:off + 128], mk, ALU.mult)
                            c.v("tensor_add", LTs[:, :], LTs[:, :], tscr[:, :])
                    for lvl in range(NL):
                        src = pb[bG] if lvl < 4 else pb[bG2]
                        off = (lvl % 4) * 128
                        mk = mkd[:, lvl * 128:(lvl + 1) * 128]
                        if lvl == 0:
                            c.v("tensor_tensor", GTs[:, :], src[:, off:off + 128],
                                mk, ALU.mult)
                        else:
                            c.v("tensor_tensor", tscr[:, :],
                                src[:, off:off + 128], mk, ALU.mult)
                            c.v("tensor_add", GTs[:, :], GTs[:, :], tscr[:, :])
                    bank_guard[bL] = ("v", c.cnt["v"])
                    bank_guard[bL2] = ("v", c.cnt["v"])
                    bank_guard[bG] = ("v", c.cnt["v"])
                    bank_guard[bG2] = ("v", c.cnt["v"])
                    bR = acq()
                    c.wait(nc.tensor, "v")
                    m = c.mmq(pb[bR][:, 0:128], KR[:, cs], Tb[:, :],
                              start=True, stop=True)
                    c.inc(m, "mm")
                    c.wait(nc.vector, "mm")
                    c.v("scalar_tensor_tensor", Rs[:, :], vpm[:, :],
                        bcseg[:, ch:ch + 1], pb[bR][:, 0:128],
                        ALU.mult, ALU.subtract)
                    bank_guard[bR] = ("v", c.cnt["v"])
                    c.v("tensor_add", zt[:, :], mkd[:, 7 * 128 - 128:7 * 128]
                        if False else id_sb[:, :], LTs[:, :])
                    c.v("tensor_sub", zn[:, :], id_sb[:, :], LTs[:, :])
                    c.wait(nc.tensor, "v")
                    tg2 = (ch + 1) % 2
                    c.gwait(nc.tensor, tb_guard[tg2])
                    m = nc.tensor.transpose(tb[:, tg2 * 512:tg2 * 512 + 128],
                                            zn[:, :], id_sb[:, :])
                    c.inc(m, "mm")
                    c.wait(nc.vector, "mm")
                    c.v("tensor_copy", t1t[:, :], tb[:, tg2 * 512:tg2 * 512 + 128])
                    tb_guard[tg2] = ("v", c.cnt["v"])
                    for it in range(6):
                        by = acq()
                        c.wait(nc.tensor, "v")
                        m = c.mmq(pb[by][:, 0:128], zt[:, :], t1t[:, :],
                                  start=True, stop=True)
                        c.inc(m, "mm")
                        c.wait(nc.vector, "mm")
                        c.v("tensor_copy", tscr[:, :], pb[by][:, 0:128])
                        c.v("tensor_copy", Us[:, :], tscr[:, :])
                        bank_guard[by] = ("v", c.cnt["v"])
                        bz = acq()
                        bzt = acq()
                        c.wait(nc.tensor, "v")
                        c.mmq(pb[bz][:, 0:128], Us[:, :], zn[:, :],
                              start=True, stop=True)
                        m = c.mmq(pb[bzt][:, 0:128], zn[:, :], Us[:, :],
                                  start=True, stop=True)
                        c.inc(m, "mm")
                        c.wait(nc.vector, "mm")
                        c.v("scalar_tensor_tensor", zn[:, :], zn[:, :], 2.0,
                            pb[bz][:, 0:128], ALU.mult, ALU.subtract)
                        c.v("scalar_tensor_tensor", t1t[:, :], t1t[:, :], 2.0,
                            pb[bzt][:, 0:128], ALU.mult, ALU.subtract)
                        bank_guard[bz] = ("v", c.cnt["v"])
                        bank_guard[bzt] = ("v", c.cnt["v"])
                    bU = acq()
                    c.wait(nc.tensor, "v")
                    m = c.mmq(pb[bU][:, 0:128], zn[:, :], Rs[:, :],
                              start=True, stop=True)
                    c.inc(m, "mm")
                    c.wait(nc.vector, "mm")
                    c.v("tensor_copy", Us[:, :], pb[bU][:, 0:128])
                    bank_guard[bU] = ("v", c.cnt["v"])
                    c.v("scalar_tensor_tensor", qkb[:, :], qcv[:, cs],
                        D ** -0.5, kcv[:, cs], ALU.mult, ALU.mult)
                    bq = acq()
                    c.wait(nc.tensor, "v")
                    m = c.mmq(pb[bq][0:1, 0:128], ones_cc[:, :], qkb[:, :],
                              start=True, stop=True)
                    c.inc(m, "mm")
                    c.wait(nc.vector, "mm")
                    c.v("tensor_copy", qkrow[:, :], pb[bq][0:1, 0:128])
                    bank_guard[bq] = ("v", c.cnt["v"])
                    c.wait(nc.tensor, "v")
                    tg3 = ch % 2
                    c.gwait(nc.tensor, tb_guard[tg3])
                    m = nc.tensor.transpose(tb[:, tg3 * 512 + 128:
                                               tg3 * 512 + 129],
                                            qkrow[:, :], id_sb[0:1, 0:1])
                    c.inc(m, "mm")
                    c.wait(nc.vector, "mm")
                    c.v("tensor_copy", qkcol[:, :],
                        tb[:, tg3 * 512 + 128:tg3 * 512 + 129])
                    tb_guard[tg3] = ("v", c.cnt["v"])
                    bO = acq()
                    c.wait(nc.tensor, "v")
                    c.mmq(pb[bO][:, 0:128], GTs[:, :], Us[:, :],
                          start=True, stop=False)
                    m = c.mmq(pb[bO][:, 0:128], QT[:, cs], Tb[:, :],
                              start=False, stop=True)
                    c.inc(m, "mm")
                    c.wait(nc.vector, "mm")
                    c.v("scalar_tensor_tensor", Os[:, :], Us[:, :],
                        qkcol[:, 0:1], pb[bO][:, 0:128], ALU.mult, ALU.add)
                    bank_guard[bO] = ("v", c.cnt["v"])
                    tg4 = (ch + 1) % 2
                    c.gwait(nc.tensor, tb_guard[tg4])
                    m = nc.tensor.transpose(tb[:, tg4 * 512 + 128:
                                               tg4 * 512 + 256],
                                            KF[:, cs], id_sb[:, :])
                    c.inc(m, "mm")
                    c.wait(nc.vector, "mm")
                    c.v("tensor_copy", kfpm[:, :],
                        tb[:, tg4 * 512 + 128:tg4 * 512 + 256])
                    tb_guard[tg4] = ("v", c.cnt["v"])
                    bT = acq()
                    c.wait(nc.tensor, "v")
                    m = c.mmq(pb[bT][:, 0:128], kfpm[:, :], Us[:, :],
                              start=True, stop=True)
                    c.inc(m, "mm")
                    c.wait(nc.vector, "mm")
                    c.v("scalar_tensor_tensor", Tf[:, :], Tf[:, :],
                        bass.AP(gam, ch, [[CPS, 128], [0, 1]]),
                        pb[bT][:, 0:128], ALU.mult, ALU.add)
                    c.v("tensor_copy", Tb[:, :], Tf[:, :])
                    bank_guard[bT] = ("v", c.cnt["v"])
                    c.v("tensor_mul", tscr[:, :], Os[:, :], Os[:, :])
                    c.v("tensor_reduce", msc[:, 0:1], tscr[:, :], AX.X, ALU.add)
                    c.wait(nc.scalar, "v")
                    c.a("activation", msc[:, 1:2], msc[:, 0:1], AF.Ln,
                        bias=c.constcol(1e-5, 128), scale=1.0 / 128)
                    c.a("activation", msc[:, 1:2], msc[:, 1:2], AF.Exp,
                        scale=-0.5)
                    gsl = gslab[:, ch * 128:(ch + 1) * 128]
                    c.a("activation", tscr[:, :], gsl, AF.Sigmoid)
                    c.wait(nc.vector, "a")
                    c.v("tensor_scalar", Os[:, :], Os[:, :], msc[:, 1:2],
                        None, ALU.mult)
                    c.v("tensor_mul", Os[:, :], Os[:, :], rms_sb[:, :])
                    c.v("tensor_mul", oslab[:, ch * 128:(ch + 1) * 128],
                        Os[:, :], tscr[:, :])
                c.wait(nc.sync, "v")
                c.dma(nc.sync, out_o[t0:t0 + SEG, hs].rearrange(
                    "(c p) n -> p c n", p=128), oslab[:, :], name="st")
                c.stwait(nc.vector)
        c.stwait(nc.sync)
    return nc


def host_prep_k4(k3_results):
    qtm, ktm, vtm, gfm, gatefm, betas = [], [], [], [], [], []
    for b in range(B):
        qtm.append(np.concatenate(
            [np.asarray(k3_results[b * 4 + i]["q_o"]) for i in range(4)], 0))
        ktm.append(np.concatenate(
            [np.asarray(k3_results[b * 4 + i]["k_o"]) for i in range(4)], 0))
        vtm.append(np.concatenate(
            [np.asarray(k3_results[b * 4 + i]["v_o"]) for i in range(4)], 0))
        gfm.append(np.concatenate(
            [np.asarray(k3_results[b * 4 + i]["g_o"]) for i in range(4)], 1))
        gatefm.append(np.concatenate(
            [np.asarray(k3_results[b * 4 + i]["gate_o"]) for i in range(4)], 1))
        betas.append(np.concatenate(
            [np.asarray(k3_results[b * 4 + i]["beta_o"]) for i in range(4)], 1))
    maps = []
    for b in range(B):
        for g in range(4):
            hs = slice(g * 512, (g + 1) * 512)
            beta_r = np.ascontiguousarray(betas[b][g * 4:(g + 1) * 4, :])
            m = dict(
                q_i=np.ascontiguousarray(qtm[b][:, hs].T),
                k_i=np.ascontiguousarray(ktm[b][:, hs].T),
                v_i=np.ascontiguousarray(vtm[b][:, hs].T),
                g_i=np.ascontiguousarray(gfm[b][hs, :]),
                gate_i=np.ascontiguousarray(gatefm[b][hs, :].T),
                beta_r=beta_r,
                beta_t2=np.ascontiguousarray(
                    beta_r.reshape(4, 32, 128).transpose(0, 2, 1)),
            )
            maps.append(m)
    return maps


def host_k4_consts(inputs):
    cvw = np.zeros((128, 48), np.float32)
    rms = np.asarray(inputs["L1_attn_rms_scale"], np.float32)
    smsk = np.zeros((128, 7 * SEG), np.float32)
    for mi, p in enumerate((2, 4, 8, 16, 32, 64, 128)):
        mrow = (np.arange(SEG) % p != 0).astype(np.float32)
        smsk[:, mi * SEG:(mi + 1) * SEG] = mrow[None, :]
    dmsk = np.zeros((128, 7 * 128), np.float32)
    for lvl, hh in enumerate(LEVELS):
        i = np.arange(128)
        mm = ((i[None, :] // hh) % 2 == 1) & \
             ((i[:, None] // hh) == (i[None, :] // hh) - 1)
        dmsk[:, lvl * 128:(lvl + 1) * 128] = mm.astype(np.float32)
    return dict(
        rmsb=np.ascontiguousarray(np.tile(rms[None, :], (128, 1))),
        smsk=smsk.astype(NBF), dmsk=dmsk.astype(NBF),
        ident=np.eye(128, dtype=NBF))


def host_k4_cvw(inputs, g):
    cvw = np.zeros((128, 48), np.float32)
    for ti, key in enumerate(("L1_attn_conv_q", "L1_attn_conv_k",
                              "L1_attn_conv_v")):
        w = np.asarray(inputs[key], np.float32)[:, 0, :]
        for h in range(HPC):
            habs = g * 4 + h
            cvw[:, ti * 16 + h * 4:ti * 16 + (h + 1) * 4] = \
                w[:, habs * 128:(habs + 1) * 128].T
    return cvw


def build_k5():
    nc = bass.Bass("TRN2", target_bir_lowering=False)
    T = TPC
    s_in = nc.dram_tensor("s_in", [E, T], BF16, kind="ExternalInput")
    x_fm = nc.dram_tensor("x_fm", [E, T], F32, kind="ExternalInput")
    g1_in = nc.dram_tensor("g1_in", [E, T], BF16, kind="ExternalInput")
    wo = nc.dram_tensor("wo", [E, E], F32, kind="ExternalInput")
    xo = nc.dram_tensor("xo", [E, T], F32, kind="ExternalOutput")

    with ExitStack() as stack:
        c = Ctx(nc, stack)
        pb = [c.ps(f"pb{i}", [128, 512]) for i in range(8)]
        bank_guard = [("", 0)] * 8
        sfm = c.sb("sfm", [128, 16 * T], BF16)
        g1t = c.sb("g1t", [128, 2 * T], BF16)
        xint = c.sb("xint", [128, 2 * T], F32)
        wbuf = c.sb("wbuf", [128, 2 * 2048], BF16)
        tmpf = c.sb("tmpf", [128, 2 * 512], F32)
        xout = c.sb("xout", [128, 2 * T], F32)
        c.dma(nc.sync, sfm[:, :], fm_slab(s_in, slice(0, E), slice(0, T)))
        par_guard = [("", 0), ("", 0)]
        xg_guard = [("", 0), ("", 0)]
        rr = [0]
        for ct in range(16):
            p = ct % 2
            c.gwait(nc.gpsimd, par_guard[p])
            c.dma(nc.gpsimd, wbuf[:, p * 2048:(p + 1) * 2048],
                  fm_slab(wo, slice(0, E), slice(ct * 128, (ct + 1) * 128)))
            c.gwait(nc.sync, xg_guard[p])
            c.dma(nc.sync, g1t[:, p * T:(p + 1) * T],
                  g1_in[ct * 128:(ct + 1) * 128, :])
            c.dma(nc.sync, xint[:, p * T:(p + 1) * T],
                  x_fm[ct * 128:(ct + 1) * 128, :])
            ldc = c.ldmark()
            for th in range(2):
                bi = rr[0]
                rr[0] = (rr[0] + 1) % 8
                s_, v_ = bank_guard[bi]
                if s_:
                    c.wait(nc.tensor, s_, v_)
                c.ldwait(nc.tensor, ldc)
                for dc in range(16):
                    m = c.mmq(
                        pb[bi][:, :],
                        wbuf[:, p * 2048 + dc * 128:p * 2048 + (dc + 1) * 128],
                        sfm[:, dc * T + th * 512:dc * T + (th + 1) * 512],
                        start=(dc == 0), stop=(dc == 15))
                c.inc(m, "mm")
                c.wait(nc.vector, "mm")
                c.ldwait(nc.vector, ldc)
                c.stwait(nc.vector)
                psl = slice(p * T + th * 512, p * T + (th + 1) * 512)
                c.v("scalar_tensor_tensor", tmpf[:, th * 512:(th + 1) * 512],
                    pb[bi][:, :], 1.0, g1t[:, psl], ALU.mult, ALU.mult)
                c.v("tensor_add", xout[:, psl],
                    tmpf[:, th * 512:(th + 1) * 512], xint[:, psl])
                bank_guard[bi] = ("v", c.cnt["v"])
            par_guard[p] = ("mm", c.cnt["mm"])
            c.wait(nc.sync, "v")
            c.dma(nc.sync, xo[ct * 128:(ct + 1) * 128, :],
                  xout[:, p * T:(p + 1) * T], name="st")
            xg_guard[p] = ("__mark__", c.stmark())
        c.stwait(nc.sync)
    return nc


def host_prep_k5(inputs, k3_results, k4_results):
    wo = np.asarray(inputs["L1_attn_Wo"], np.float32)
    sfull = []
    for b in range(B):
        s_pm = np.concatenate(
            [np.asarray(k4_results[b * 4 + g]["out_o"]) for g in range(4)],
            axis=1)
        sfull.append(np.ascontiguousarray(s_pm.T))
    maps = []
    for ci in range(NCORE):
        b, qq = divmod(ci, 4)
        tsl = slice(qq * TPC, (qq + 1) * TPC)
        maps.append(dict(
            s_in=np.ascontiguousarray(sfull[b][:, tsl]),
            x_fm=np.asarray(k3_results[ci]["x1_o"]),
            g1_in=np.asarray(k3_results[ci]["gs1_o"]),
            wo=wo))
    return maps


from concourse.bass_utils import run_bass_kernel_spmd


def _run(nc, maps):
    return run_bass_kernel_spmd(nc, maps, core_ids=list(range(NCORE))).results


def kernel(**inputs):
    r1 = _run(build_k1(), host_prep_k1(inputs))
    r2 = _run(build_k2(), host_prep_k2(r1))
    r3 = _run(build_k3(), host_prep_k3(inputs, r1, r2))
    maps4 = host_prep_k4(r3)
    consts = host_k4_consts(inputs)
    for i, m in enumerate(maps4):
        m.update(consts)
        m["cvw"] = host_k4_cvw(inputs, i % 4)
    r4 = _run(build_k4(), maps4)
    r5 = _run(build_k5(), host_prep_k5(inputs, r3, r4))
    out = np.zeros((B, S, E), np.float32)
    for ci in range(NCORE):
        b, qq = divmod(ci, 4)
        out[b, qq * TPC:(qq + 1) * TPC, :] = np.asarray(r5[ci]["xo"]).T
    return out
